# version 07
# baseline (speedup 1.0000x reference)
import numpy as np
from contextlib import ExitStack

import concourse.bass as bass
import concourse.mybir as mybir
from concourse.bass_utils import run_bass_kernel_spmd

F32 = mybir.dt.float32
BF16 = mybir.dt.bfloat16
I32 = mybir.dt.int32
AF = mybir.ActivationFunctionType
ALU = mybir.AluOpType

H = 4096
II = 11008
BATCH = 4
S = 2048
T = 1024
HC = H // 128
IC = II // 128
NQH = 32
NKV = 8
D = 128
GQ = NQH // NKV
NEG_STAB = -32.0
QSCALE = float(D) ** -0.5
EPS = 1e-5

ENGS = ("sync", "gpsimd", "tensor", "scalar", "vector")


class Sem:
    def __init__(self, handle):
        self.h = handle
        self.n = 0

    def plan(self, k):
        self.n += k
        return self.n


class Ph:
    def __init__(self):
        self.q = {e: [] for e in ENGS}

    def on(self, e, fn):
        self.q[e].append(fn)

    def flush(self, block):
        for e in ENGS:
            fns = self.q[e]
            if not fns:
                continue

            def runner(eng, fns=fns):
                for f in fns:
                    f(eng)

            getattr(block, e)(runner)
        self.q = {e: [] for e in ENGS}


def _w(eng, waits):
    for s, v in waits:
        if v > 0:
            eng.wait_ge(s.h, v)


def build_nc(phase_limit=99):
    nc = bass.Bass("TRN2", target_bir_lowering=False)

    x_own = nc.dram_tensor("x_own", [128, HC * T], F32, kind="ExternalInput").ap()
    x_par = nc.dram_tensor("x_par", [128, HC * T], F32, kind="ExternalInput").ap()
    gate_w = nc.dram_tensor("gate_w", [IC, 128, HC * 128], I32, kind="ExternalInput").ap()
    up_w = nc.dram_tensor("up_w", [IC, 128, HC * 128], I32, kind="ExternalInput").ap()
    down_w = nc.dram_tensor("down_w", [8, 2, 128, 43 * 512], I32, kind="ExternalInput").ap()
    wq_w = nc.dram_tensor("wq_w", [NQH, 128, HC * 128], F32, kind="ExternalInput").ap()
    wk_w = nc.dram_tensor("wk_w", [NKV, 128, HC * 128], F32, kind="ExternalInput").ap()
    wv_w = nc.dram_tensor("wv_w", [NKV, 128, HC * 128], F32, kind="ExternalInput").ap()
    wo_w = nc.dram_tensor("wo_w", [HC, 128, HC * 128], F32, kind="ExternalInput").ap()
    gate_sc = nc.dram_tensor("gate_sc", [128, IC], F32, kind="ExternalInput").ap()
    up_sc = nc.dram_tensor("up_sc", [128, IC], F32, kind="ExternalInput").ap()
    down_sc = nc.dram_tensor("down_sc", [128, HC], F32, kind="ExternalInput").ap()
    bv_c = nc.dram_tensor("bv_c", [128, NKV], F32, kind="ExternalInput").ap()
    ident_in = nc.dram_tensor("ident_in", [128, 128], F32, kind="ExternalInput").ap()

    out = nc.dram_tensor("out", [128, HC * T], F32, kind="ExternalOutput").ap()

    r_own = nc.dram_tensor("r_own", [128, HC * T], F32).ap()
    r_par = nc.dram_tensor("r_par", [128, HC * T], F32).ap()
    inter_h = nc.dram_tensor("inter_h", [2, 128, IC * 512], BF16).ap()
    q_h = nc.dram_tensor("q_h", [NQH, 128, T], BF16).ap()

    es = ExitStack()
    with es:
        ones_colf = es.enter_context(nc.sbuf_tensor("ones_colf", [128, 1], F32))
        ones_rowf = es.enter_context(nc.sbuf_tensor("ones_rowf", [1, 128], F32))
        ones_colb = es.enter_context(nc.sbuf_tensor("ones_colb", [128, 1], BF16))
        ones_sqb = es.enter_context(nc.sbuf_tensor("ones_sqb", [128, 128], BF16))
        neg32 = es.enter_context(nc.sbuf_tensor("neg32", [128, 1], F32))
        gsc = es.enter_context(nc.sbuf_tensor("gsc", [128, IC], F32))
        usc = es.enter_context(nc.sbuf_tensor("usc", [128, IC], F32))
        dsc = es.enter_context(nc.sbuf_tensor("dsc", [128, HC], F32))
        hdsc = es.enter_context(nc.sbuf_tensor("hdsc", [128, HC], F32))
        bvc = es.enter_context(nc.sbuf_tensor("bvc", [128, NKV], F32))
        identb = es.enter_context(nc.sbuf_tensor("identb", [128, 128], BF16))
        mrow = es.enter_context(nc.sbuf_tensor("mrow", [1, T], F32))
        vrow = es.enter_context(nc.sbuf_tensor("vrow", [1, T], F32))
        srow = es.enter_context(nc.sbuf_tensor("srow", [1, T], F32))
        rrow = es.enter_context(nc.sbuf_tensor("rrow", [1, T], F32))

        all_sems = []

        def mksem(name):
            s = Sem(es.enter_context(nc.semaphore(name)))
            all_sems.append(s)
            return s

        sPE = mksem("sPE")
        sSE = mksem("sSE")
        sVE = mksem("sVE")
        dPL = mksem("dPL")
        dID = mksem("dID")
        dLN = [mksem(f"dLN{i}") for i in range(2)]
        dWP = [mksem(f"dWP{i}") for i in range(2)]
        dIT = [mksem(f"dIT{i}") for i in range(2)]
        dDB = [mksem(f"dDB{i}") for i in range(4)]
        dIB = [mksem(f"dIB{i}") for i in range(4)]
        dRL = mksem("dRL")
        dRS = mksem("dRS")
        dWS = [mksem(f"dWS{i}") for i in range(2)]
        dQS = [mksem(f"dQS{i}") for i in range(2)]
        dQH = [mksem(f"dQH{i}") for i in range(2)]
        dR2L = [mksem(f"dR2L{i}") for i in range(2)]
        dR2S = [mksem(f"dR2S{i}") for i in range(2)]

        block = es.enter_context(nc.Block())
        ph = Ph()
        phase_no = [0]
        _uid = [0]

        def nm(base):
            _uid[0] += 1
            return f"{base}_{_uid[0]}"

        def join_all():
            tgts = tuple((s, s.n) for s in all_sems)
            for e in ENGS:
                ph.on(e, lambda eng, tg=tgts: _w(eng, tg))

        def next_phase():
            phase_no[0] += 1
            return phase_no[0] <= phase_limit

        def p_prologue():
            def f_ve(eng):
                eng.memset(ones_colf[:], 1.0)
                eng.memset(ones_rowf[:], 1.0)
                eng.memset(ones_colb[:], 1.0)
                eng.memset(ones_sqb[:], 1.0)
                eng.memset(neg32[:], NEG_STAB)
            ph.on("vector", f_ve)
            for dst, src in ((gsc, gate_sc), (usc, up_sc), (dsc, down_sc), (bvc, bv_c)):
                dPL.plan(16)
                ph.on("sync", lambda eng, d=dst, s=src: eng.dma_start(out=d[:], in_=s).then_inc(dPL.h, 16))
            dID.plan(16)
            ph.on("gpsimd", lambda eng: eng.dma_start(out=identb[:], in_=ident_in).then_inc(dID.h, 16))
            vall = dPL.n
            sVE.plan(1)
            def f_ve2(eng, vall=vall):
                eng.wait_ge(dPL.h, vall)
                eng.tensor_scalar_mul(hdsc[:], dsc[:], 0.5).then_inc(sVE.h, 1)
            ph.on("vector", f_ve2)

        def p_ln(src, xln, lnbuf, lnsq, lnxb, lnqb, pstat, pbc, mb_sb, rb_sb):
            stat_pe = []
            cast_ve = []
            sq_se = []
            for hc in range(HC):
                par = hc % 2
                wait = []
                if hc >= 2:
                    wait.append((sVE, cast_ve[hc - 2]))
                    wait.append((sSE, sq_se[hc - 2]))
                v = dLN[par].plan(16)
                ph.on("sync", lambda eng, w=tuple(wait), s=src, hc=hc, b=lnbuf[par], sm=dLN[par]:
                      (_w(eng, w), eng.dma_start(out=b[:], in_=s[:, hc * T:(hc + 1) * T]).then_inc(sm.h, 16)))
                waitc = [(dLN[par], v)]
                if hc >= 2:
                    waitc.append((sPE, stat_pe[hc - 2]))
                tv = sVE.plan(1)
                ph.on("vector", lambda eng, w=tuple(waitc), b=lnbuf[par], xb=lnxb[par]:
                      (_w(eng, w), eng.tensor_copy(xb[:], b[:]).then_inc(sVE.h, 1)))
                cast_ve.append(tv)
                ts = sSE.plan(1)
                ph.on("scalar", lambda eng, w=tuple(waitc), b=lnbuf[par], qb=lnqb[par]:
                      (_w(eng, w), eng.activation(qb[:], b[:], AF.Square).then_inc(sSE.h, 1)))
                sq_se.append(ts)
                tp = sPE.plan(1)
                def f_pe(eng, tv=tv, ts=ts, hc=hc, xb=lnxb[par], qb=lnqb[par]):
                    eng.wait_ge(sVE.h, tv)
                    eng.wait_ge(sSE.h, ts)
                    st = hc == 0
                    sp = hc == HC - 1
                    for half in range(2):
                        eng.matmul(pstat[:, half * 512:(half + 1) * 512], ones_sqb[:],
                                   xb[:, half * 512:(half + 1) * 512], start=st, stop=sp)
                    for half in range(2):
                        i = eng.matmul(pstat[:, T + half * 512:T + (half + 1) * 512], ones_sqb[:],
                                       qb[:, half * 512:(half + 1) * 512], start=st, stop=sp)
                    i.then_inc(sPE.h, 1)
                ph.on("tensor", f_pe)
                stat_pe.append(tp)
            tp_all = sPE.n
            v1 = sVE.plan(1); sVE.plan(1); v3 = sVE.plan(1); v4 = sVE.plan(1); v5 = sVE.plan(1)
            def f_fin_ve(eng, tp_all=tp_all, v1=v1, v3=v3, v4=v4):
                eng.wait_ge(sPE.h, tp_all)
                eng.tensor_scalar_mul(mrow[:], pstat[:1, 0:T], 1.0 / H).then_inc(sVE.h, 1)
                eng.tensor_scalar_mul(vrow[:], pstat[:1, T:2 * T], 1.0 / H).then_inc(sVE.h, 1)
                eng.wait_ge(sVE.h, v1)
                eng.tensor_mul(srow[:], mrow[:], mrow[:]).then_inc(sVE.h, 1)
                eng.wait_ge(sVE.h, v3)
                eng.tensor_scalar(srow[:], srow[:], -1.0, EPS, ALU.mult, ALU.add).then_inc(sVE.h, 1)
                eng.wait_ge(sVE.h, v4)
                eng.tensor_add(vrow[:], vrow[:], srow[:]).then_inc(sVE.h, 1)
            ph.on("vector", f_fin_ve)
            ts1 = sSE.plan(1)
            ph.on("scalar", lambda eng, v5=v5: (eng.wait_ge(sVE.h, v5),
                  eng.activation(srow[:], vrow[:], AF.Sqrt).then_inc(sSE.h, 1)))
            tv6 = sVE.plan(1)
            ph.on("vector", lambda eng, ts1=ts1: (eng.wait_ge(sSE.h, ts1),
                  eng.reciprocal(rrow[:], srow[:]).then_inc(sVE.h, 1)))
            tp_bc = sPE.plan(1)
            def f_bc(eng, tv6=tv6):
                eng.wait_ge(sVE.h, tv6)
                for half in range(2):
                    eng.matmul(pbc[:, half * 512:(half + 1) * 512], ones_rowf[:],
                               mrow[:, half * 512:(half + 1) * 512], start=True, stop=True)
                for half in range(2):
                    i = eng.matmul(pbc[:, T + half * 512:T + (half + 1) * 512], ones_rowf[:],
                                   rrow[:, half * 512:(half + 1) * 512], start=True, stop=True)
                i.then_inc(sPE.h, 1)
            ph.on("tensor", f_bc)
            sSE.plan(1)
            cp_t = sSE.plan(1)
            def f_bcp(eng, tp_bc=tp_bc):
                eng.wait_ge(sPE.h, tp_bc)
                eng.copy(mb_sb[:], pbc[:, 0:T]).then_inc(sSE.h, 1)
                eng.copy(rb_sb[:], pbc[:, T:2 * T]).then_inc(sSE.h, 1)
            ph.on("scalar", f_bcp)
            app_ve = []
            for hc in range(HC):
                par = hc % 2
                wait = [(sPE, tp_all)] if hc < 2 else [(sVE, app_ve[hc - 2])]
                v = dLN[par].plan(16)
                ph.on("sync", lambda eng, w=tuple(wait), s=src, hc=hc, b=lnbuf[par], sm=dLN[par]:
                      (_w(eng, w), eng.dma_start(out=b[:], in_=s[:, hc * T:(hc + 1) * T]).then_inc(sm.h, 16)))
                va = sVE.plan(1)
                vb = sVE.plan(1)
                def f_app(eng, v=v, hc=hc, b=lnbuf[par], q=lnsq[par], cp_t=cp_t, va=va, sm=dLN[par]):
                    eng.wait_ge(sm.h, v)
                    eng.wait_ge(sSE.h, cp_t)
                    eng.tensor_sub(q[:], b[:], mb_sb[:]).then_inc(sVE.h, 1)
                    eng.wait_ge(sVE.h, va)
                    eng.tensor_tensor(xln[:, hc * T:(hc + 1) * T], q[:], rb_sb[:], ALU.mult).then_inc(sVE.h, 1)
                ph.on("vector", f_app)
                app_ve.append(vb)
            return app_ve, cp_t

        def p_a1(xln, gbuf, ubuf, sscr, itile, pg, pu, xg=None, cp_t=0):
            pe_done, ve_done, se_done, st_done = [], [], [], []
            for ic in range(IC):
                par = ic % 2
                waitw = [(sPE, pe_done[ic - 2])] if ic >= 2 else []
                dWP[par].plan(16)
                vu = dWP[par].plan(16)
                def f_dma(eng, w=tuple(waitw), ic=ic, g=gbuf[par], u=ubuf[par], sm=dWP[par]):
                    _w(eng, w)
                    eng.dma_start(out=g[:], in_=gate_w[ic]).then_inc(sm.h, 16)
                    eng.dma_start(out=u[:], in_=up_w[ic]).then_inc(sm.h, 16)
                ph.on("gpsimd", f_dma)
                waitp = [(dWP[par], vu)]
                if ic >= 2:
                    waitp.append((sSE, se_done[ic - 2]))
                    waitp.append((sVE, ve_done[ic - 2]))
                elif cp_t:
                    waitp.append((sSE, cp_t))
                gates = xg if (xg is not None and ic < 2) else None
                tg = sPE.plan(1)
                tu = sPE.plan(1)
                def f_mm(eng, w=tuple(waitp), g=gbuf[par], u=ubuf[par], pg_=pg[par], pu_=pu[par], gates=gates):
                    _w(eng, w)
                    for hc in range(HC):
                        st = hc == 0
                        sp = hc == HC - 1
                        if gates is not None:
                            eng.wait_ge(sVE.h, gates[hc])
                        lhs = g[:, hc * 128:(hc + 1) * 128]
                        eng.matmul(pg_[:, 0:512], lhs, xln[:, hc * T:hc * T + 512], start=st, stop=sp)
                        i = eng.matmul(pg_[:, 512:1024], lhs, xln[:, hc * T + 512:(hc + 1) * T], start=st, stop=sp)
                    i.then_inc(sPE.h, 1)
                    for hc in range(HC):
                        st = hc == 0
                        sp = hc == HC - 1
                        lhs = u[:, hc * 128:(hc + 1) * 128]
                        eng.matmul(pu_[:, 0:512], lhs, xln[:, hc * T:hc * T + 512], start=st, stop=sp)
                        i = eng.matmul(pu_[:, 512:1024], lhs, xln[:, hc * T + 512:(hc + 1) * T], start=st, stop=sp)
                    i.then_inc(sPE.h, 1)
                ph.on("tensor", f_mm)
                pe_done.append(tu)
                ts = sSE.plan(1)
                def f_silu(eng, tg=tg, ic=ic, pg_=pg[par], sc=sscr[par]):
                    eng.wait_ge(sPE.h, tg)
                    eng.activation(sc[:], pg_[:], AF.Silu, scale=gsc[:, ic:ic + 1]).then_inc(sSE.h, 1)
                ph.on("scalar", f_silu)
                se_done.append(ts)
                waitv = [(sPE, tu), (sSE, ts)]
                if ic >= 2:
                    waitv.append((dIT[par], st_done[ic - 2]))
                tv = sVE.plan(1)
                def f_stt(eng, w=tuple(waitv), ic=ic, pu_=pu[par], sc=sscr[par], it=itile[par]):
                    _w(eng, w)
                    eng.scalar_tensor_tensor(it[:], pu_[:], usc[:, ic:ic + 1], sc[:], ALU.mult, ALU.mult).then_inc(sVE.h, 1)
                ph.on("vector", f_stt)
                ve_done.append(tv)
                dIT[par].plan(16)
                vs = dIT[par].plan(16)
                def f_st(eng, tv=tv, ic=ic, it=itile[par], sm=dIT[par]):
                    eng.wait_ge(sVE.h, tv)
                    eng.dma_start(out=inter_h[0][:, ic * 512:(ic + 1) * 512], in_=it[:, 0:512]).then_inc(sm.h, 16)
                    eng.dma_start(out=inter_h[1][:, ic * 512:(ic + 1) * 512], in_=it[:, 512:1024]).then_inc(sm.h, 16)
                ph.on("scalar", f_st)
                st_done.append(vs)

        QSTART = (0, 22, 43, 65)
        QLEN = (22, 21, 22, 21)

        def q_of(ic):
            return 0 if ic < 22 else 1 if ic < 43 else 2 if ic < 65 else 3

        def p_a2(r_src, r_dst, dbuf, ibuf, rt, pd):
            NG = 22
            prev_store = 0
            last_ig = [0, 0, 0, 0]
            q_first = [None] * 4
            q_last = [0] * 4
            for g in range(NG):
                n = 4 if g < 21 else 2
                for j in range(n):
                    q = q_of(g * 4 + j)
                    if q_first[q] is None:
                        q_first[q] = g
                    q_last[q] = g
            dq_done = [0, 0, 0, 0]
            prev_ev = [0, 0]
            for s8 in range(8):
                wdq = []
                for q in range(4):
                    hf = 0 if q < 2 else 1
                    off = QSTART[q] - (0 if q < 2 else 43)
                    wait = [(sPE, dq_done[q])]
                    v = dDB[q].plan(16)
                    ph.on("gpsimd", lambda eng, w=tuple(wait), s8=s8, q=q, hf=hf, off=off, sm=dDB[q]:
                          (_w(eng, w), eng.dma_start(out=dbuf[q][:, 0:QLEN[q] * 512], in_=down_w[s8, hf][:, off * 512:(off + QLEN[q]) * 512]).then_inc(sm.h, 16)))
                    wdq.append(v)
                wait = [(dRS, prev_store)] if s8 > 0 else []
                for oi in range(4):
                    dRL.plan(16)
                    ph.on("scalar", lambda eng, w=tuple(wait), s8=s8, oi=oi, r=rt[oi]:
                          (_w(eng, w), eng.dma_start(out=r[:], in_=r_src[:, (s8 * 4 + oi) * T:(s8 * 4 + oi + 1) * T]).then_inc(dRL.h, 16)))
                rl_all = dRL.n
                ev_t = [0, 0]
                ig_done = []
                for th in range(2):
                    for g in range(NG):
                        gi = th * NG + g
                        par = gi % 4
                        n = 4 if g < 21 else 2
                        wait = [(sPE, ig_done[gi - 4])] if gi >= 4 else [(sPE, last_ig[gi])]
                        v = dIB[par].plan(16)
                        ph.on("sync", lambda eng, w=tuple(wait), th=th, g=g, n=n, b=ibuf[par], sm=dIB[par]:
                              (_w(eng, w), eng.dma_start(out=b[:, 0:n * 512], in_=inter_h[th][:, g * 4 * 512:(g * 4 + n) * 512]).then_inc(sm.h, 16)))
                        tp = sPE.plan(1)
                        def f_mm(eng, v=v, th=th, g=g, n=n, b=ibuf[par], wdq=tuple(wdq), par=par,
                                 pe0=prev_ev[th], qf=tuple(q_first)):
                            eng.wait_ge(dIB[par].h, v)
                            if g == 0 and pe0:
                                eng.wait_ge(sVE.h, pe0)
                            if th == 0:
                                for q in range(4):
                                    if qf[q] == g:
                                        eng.wait_ge(dDB[q].h, wdq[q])
                            for j in range(n):
                                ic = g * 4 + j
                                q = q_of(ic)
                                off = ic - QSTART[q]
                                st = ic == 0
                                sp = ic == IC - 1
                                for oi in range(4):
                                    lhs = dbuf[q][:, (off * 4 + oi) * 128:(off * 4 + oi + 1) * 128]
                                    i = eng.matmul(pd[:, oi * T + th * 512:oi * T + th * 512 + 512], lhs,
                                                   b[:, j * 512:(j + 1) * 512], start=st, stop=sp)
                            i.then_inc(sPE.h, 1)
                        ph.on("tensor", f_mm)
                        ig_done.append(tp)
                    tv = sVE.plan(1)
                    def f_ev(eng, tall=ig_done[-1], rl_all=rl_all, s8=s8, th=th):
                        eng.wait_ge(sPE.h, tall)
                        eng.wait_ge(dRL.h, rl_all)
                        for oi in range(4):
                            i = eng.scalar_tensor_tensor(rt[oi][:, th * 512:(th + 1) * 512],
                                                         pd[:, oi * T + th * 512:oi * T + (th + 1) * 512],
                                                         hdsc[:, s8 * 4 + oi:s8 * 4 + oi + 1],
                                                         rt[oi][:, th * 512:(th + 1) * 512],
                                                         ALU.mult, ALU.add)
                        i.then_inc(sVE.h, 1)
                    ph.on("vector", f_ev)
                    ev_t[th] = tv
                last_ig = [ig_done[40], ig_done[41], ig_done[42], ig_done[43]]
                dq_done = [ig_done[NG + q_last[q]] for q in range(4)]
                prev_ev = list(ev_t)
                for oi in range(4):
                    dRS.plan(16)
                    ph.on("scalar", lambda eng, tv=ev_t[1], s8=s8, oi=oi, r=rt[oi]:
                          (eng.wait_ge(sVE.h, tv),
                           eng.dma_start(out=r_dst[:, (s8 * 4 + oi) * T:(s8 * 4 + oi + 1) * T], in_=r[:]).then_inc(dRS.h, 16)))
                prev_store = dRS.n

        def proj_loop(xln, w_dram, nout, wslab, pk, slab_free, psum_free, on_out, xg=None, cp_t=0):
            mm_done = []
            ev_done = []
            for j in range(nout):
                par = j % 2
                wait = [slab_free(j)] if j < 2 else [(sPE, mm_done[j - 2])]
                v = dWS[par].plan(16)
                ph.on("gpsimd", lambda eng, w=tuple(x for x in wait if x), j=j, b=wslab[par], sm=dWS[par]:
                      (_w(eng, w), eng.dma_start(out=b[:], in_=w_dram[j]).then_inc(sm.h, 16)))
                waitp = [(dWS[par], v)]
                ep = ev_done[j - 2] if j >= 2 else psum_free(j)
                if ep:
                    waitp.append(ep)
                if j < 2 and cp_t:
                    waitp.append((sSE, cp_t))
                gates = xg if (xg is not None and j < 2) else None
                tp = sPE.plan(1)
                def f_mm(eng, w=tuple(waitp), b=wslab[par], p=pk[par], gates=gates):
                    _w(eng, w)
                    for hc in range(HC):
                        st = hc == 0
                        sp = hc == HC - 1
                        if gates is not None:
                            eng.wait_ge(sVE.h, gates[hc])
                        lhs = b[:, hc * 128:(hc + 1) * 128]
                        eng.matmul(p[:, 0:512], lhs, xln[:, hc * T:hc * T + 512], start=st, stop=sp)
                        i = eng.matmul(p[:, 512:1024], lhs, xln[:, hc * T + 512:(hc + 1) * T], start=st, stop=sp)
                    i.then_inc(sPE.h, 1)
                ph.on("tensor", f_mm)
                mm_done.append(tp)
                ev_done.append(on_out(j, tp))
            return mm_done, ev_done

        def p_b1(xln, hf, do_q, wslab, pk, vfm, pt, qtile, K_sb, V_sb, xg=None, cp_t=0):
            def k_out(ko, tp):
                ts = sSE.plan(1)
                ph.on("scalar", lambda eng, tp=tp, ko=ko, p=pk[ko % 2]:
                      (eng.wait_ge(sPE.h, tp),
                       eng.copy(K_sb[:, (ko * 2 + hf) * T:(ko * 2 + hf + 1) * T], p[:]).then_inc(sSE.h, 1)))
                return (sSE, ts)
            k_mm, k_ev = proj_loop(xln, wk_w, NKV, wslab, pk,
                                   lambda j: None, lambda j: None, k_out, xg=xg, cp_t=cp_t)

            def v_out(vo, tp):
                ts = sSE.plan(1)
                def f_ev(eng, tp=tp, vo=vo, p=pk[vo % 2], f=vfm[vo % 2]):
                    eng.wait_ge(sPE.h, tp)
                    eng.activation(f[:], p[:], AF.Identity, bias=bvc[:, vo:vo + 1]).then_inc(sSE.h, 1)
                ph.on("scalar", f_ev)
                last = None
                tr = []
                for tt in range(8):
                    waitt = [(sSE, ts)]
                    if tt >= 2:
                        waitt.append((sSE, tr[tt - 2]))
                    tpx = sPE.plan(1)
                    def f_tr(eng, w=tuple(waitt), tt=tt, f=vfm[vo % 2], p=pt[tt % 2]):
                        _w(eng, w)
                        eng.transpose(p[:], f[:, tt * 128:(tt + 1) * 128], identb[:]).then_inc(sPE.h, 1)
                    ph.on("tensor", f_tr)
                    tse = sSE.plan(1)
                    def f_cp(eng, tpx=tpx, tt=tt, vo=vo, p=pt[tt % 2]):
                        eng.wait_ge(sPE.h, tpx)
                        eng.copy(V_sb[:, (hf * 8 + tt) * T + vo * 128:(hf * 8 + tt) * T + (vo + 1) * 128], p[:]).then_inc(sSE.h, 1)
                    ph.on("scalar", f_cp)
                    tr.append(tse)
                    last = tse
                return (sSE, last)
            v_mm, v_ev = proj_loop(xln, wv_w, NKV, wslab, pk,
                                   lambda j: (sPE, k_mm[-1]),
                                   lambda j: k_ev[NKV - 2 + (j % 2)], v_out)

            if do_q:
                qst = [0, 0]
                def q_out(qo, tp):
                    par = qo % 2
                    waitv = [(sPE, tp)]
                    if qo >= 2:
                        waitv.append((dQS[par], qst[par]))
                    tv = sVE.plan(1)
                    def f_sc(eng, w=tuple(waitv), p=pk[par], qt=qtile[par]):
                        _w(eng, w)
                        eng.tensor_scalar_mul(qt[:], p[:], QSCALE).then_inc(sVE.h, 1)
                    ph.on("vector", f_sc)
                    qst[par] = dQS[par].plan(16)
                    ph.on("scalar", lambda eng, tv=tv, qo=qo, qt=qtile[par], sm=dQS[par]:
                          (eng.wait_ge(sVE.h, tv),
                           eng.dma_start(out=q_h[qo], in_=qt[:]).then_inc(sm.h, 16)))
                    return (sVE, tv)
                proj_loop(xln, wq_w, NQH, wslab, pk,
                          lambda j: (sPE, v_mm[-1]),
                          lambda j: v_ev[NKV - 2 + (j % 2)], q_out)

        def p_b2(K_sb, V_sb, ctx_sb, qh, probs, rbsb, psc, prow, pctx):
            qld = {}
            for h in range(2):
                v = dQH[h % 2].plan(16)
                ph.on("sync", lambda eng, h=h, b=qh[h % 2], sm=dQH[h % 2]:
                      eng.dma_start(out=b[:], in_=q_h[h]).then_inc(sm.h, 16))
                qld[h] = v
            prev_exp = [0] * 16
            prev_su = [0] * 16
            prev_ctx_ve = 0
            prev_rb_copy = 0
            pend_rb = None
            for h in range(NQH):
                g = h // GQ
                sc_t = [0] * 16
                su_t = [0] * 16
                exp_t = [0] * 16

                def plan_scores(t, h=h, g=g, prev_exp=prev_exp, prev_su=prev_su):
                    waitp = []
                    if t == 0:
                        waitp.append((dQH[h % 2], qld[h]))
                    pe_free = prev_exp[14 + t] if t < 2 else exp_t[t - 2]
                    if pe_free:
                        waitp.append((sSE, pe_free))
                    tp = sPE.plan(1)
                    def f_sc(eng, w=tuple(waitp), h=h, g=g, t=t, p=psc[t % 2]):
                        _w(eng, w)
                        base = (g * 2 + (t // 8)) * T + (t % 8) * 128
                        lhs = K_sb[:, base:base + 128]
                        eng.matmul(p[:, 0:512], lhs, qh[h % 2][:, 0:512], start=True, stop=True)
                        eng.matmul(p[:, 512:1024], lhs, qh[h % 2][:, 512:1024], start=True, stop=True).then_inc(sPE.h, 1)
                    ph.on("tensor", f_sc)
                    sc_t[t] = tp
                    waite = [(sPE, tp)]
                    pw = prev_su[12 + t] if t < 4 else su_t[t - 4]
                    if pw:
                        waite.append((sPE, pw))
                    ts = sSE.plan(1)
                    def f_exp(eng, w=tuple(waite), t=t, p=psc[t % 2], pr=probs[t % 4]):
                        _w(eng, w)
                        eng.activation(pr[:], p[:], AF.Exp, bias=neg32[:]).then_inc(sSE.h, 1)
                    ph.on("scalar", f_exp)
                    exp_t[t] = ts

                def plan_sum(t, h=h, g=g, prev_ctx_ve=prev_ctx_ve):
                    waitq = [(sSE, exp_t[t])]
                    if t == 0 and prev_ctx_ve:
                        waitq.append((sVE, prev_ctx_ve))
                    tq = sPE.plan(1)
                    def f_sum(eng, w=tuple(waitq), t=t, g=g, pr=probs[t % 4]):
                        _w(eng, w)
                        st = t == 0
                        sp = t == 15
                        eng.matmul(prow[:, 0:512], ones_sqb[:], pr[:, 0:512], start=st, stop=sp)
                        eng.matmul(prow[:, 512:1024], ones_sqb[:], pr[:, 512:1024], start=st, stop=sp)
                        lhs = V_sb[:, t * T + g * 128:t * T + (g + 1) * 128]
                        eng.matmul(pctx[:, 0:512], lhs, pr[:, 0:512], start=st, stop=sp)
                        eng.matmul(pctx[:, 512:1024], lhs, pr[:, 512:1024], start=st, stop=sp).then_inc(sPE.h, 1)
                    ph.on("tensor", f_sum)
                    su_t[t] = tq

                plan_scores(0)
                plan_scores(1)
                if pend_rb is not None:
                    pend_rb()
                    pend_rb = None
                for t in range(16):
                    if t + 2 < 16:
                        plan_scores(t + 2)
                    plan_sum(t)
                if h + 2 < NQH:
                    v = dQH[h % 2].plan(16)
                    ph.on("sync", lambda eng, w=sc_t[15], h2=h + 2, b=qh[h % 2], sm=dQH[h % 2]:
                          (eng.wait_ge(sPE.h, w), eng.dma_start(out=b[:], in_=q_h[h2]).then_inc(sm.h, 16)))
                    qld[h + 2] = v
                tv1 = sVE.plan(1)
                ph.on("vector", lambda eng, tsum=su_t[15], pcv=prev_ctx_ve: (eng.wait_ge(sPE.h, tsum),
                      eng.reciprocal(rrow[:], prow[:1, 0:T]).then_inc(sVE.h, 1)))
                tpb = sPE.plan(1)
                tse = sSE.plan(1)
                tv2 = sVE.plan(1)
                def mk_rb(tv1=tv1, tpb=tpb, tse=tse, tv2=tv2, h=h, prc=prev_rb_copy, pcv=prev_ctx_ve):
                    def f_rb(eng):
                        eng.wait_ge(sVE.h, tv1)
                        if prc:
                            eng.wait_ge(sSE.h, prc)
                        eng.matmul(prow[:, 0:512], ones_rowf[:], rrow[:, 0:512], start=True, stop=True)
                        eng.matmul(prow[:, 512:1024], ones_rowf[:], rrow[:, 512:1024], start=True, stop=True).then_inc(sPE.h, 1)
                    ph.on("tensor", f_rb)
                    def f_cp(eng):
                        eng.wait_ge(sPE.h, tpb)
                        if pcv:
                            eng.wait_ge(sVE.h, pcv)
                        eng.copy(rbsb[:], prow[:]).then_inc(sSE.h, 1)
                    ph.on("scalar", f_cp)
                    def f_cev(eng):
                        eng.wait_ge(sSE.h, tse)
                        eng.tensor_tensor(ctx_sb[:, h * T:(h + 1) * T], pctx[:], rbsb[:], ALU.mult).then_inc(sVE.h, 1)
                    ph.on("vector", f_cev)
                pend_rb = mk_rb()
                prev_exp = list(exp_t)
                prev_su = list(su_t)
                prev_ctx_ve = tv2
                prev_rb_copy = tse
            if pend_rb is not None:
                pend_rb()

        def p_b3(ctx_sb, wslab, rt2, pw):
            o_mm, o_ev = [], []
            o_st = [0, 0]
            for oc in range(HC):
                par = oc % 2
                wait = [(sPE, o_mm[oc - 2])] if oc >= 2 else []
                v = dWS[par].plan(16)
                ph.on("gpsimd", lambda eng, w=tuple(wait), oc=oc, b=wslab[par], sm=dWS[par]:
                      (_w(eng, w), eng.dma_start(out=b[:], in_=wo_w[oc]).then_inc(sm.h, 16)))
                waitr = [(dR2S[par], o_st[par])] if oc >= 2 else []
                vr = dR2L[par].plan(16)
                ph.on("sync", lambda eng, w=tuple(waitr), oc=oc, r=rt2[par], sm=dR2L[par]:
                      (_w(eng, w), eng.dma_start(out=r[:], in_=r_own[:, oc * T:(oc + 1) * T]).then_inc(sm.h, 16)))
                waitp = [(dWS[par], v)]
                if oc >= 2:
                    waitp.append((sVE, o_ev[oc - 2]))
                tp = sPE.plan(1)
                def f_mm(eng, w=tuple(waitp), b=wslab[par], p=pw[par]):
                    _w(eng, w)
                    for hc in range(HC):
                        st = hc == 0
                        sp = hc == HC - 1
                        lhs = b[:, hc * 128:(hc + 1) * 128]
                        eng.matmul(p[:, 0:512], lhs, ctx_sb[:, hc * T:hc * T + 512], start=st, stop=sp)
                        i = eng.matmul(p[:, 512:1024], lhs, ctx_sb[:, hc * T + 512:(hc + 1) * T], start=st, stop=sp)
                    i.then_inc(sPE.h, 1)
                ph.on("tensor", f_mm)
                o_mm.append(tp)
                tv = sVE.plan(1)
                def f_add(eng, tp=tp, vr=vr, p=pw[par], r=rt2[par], sm=dR2L[par]):
                    eng.wait_ge(sPE.h, tp)
                    eng.wait_ge(sm.h, vr)
                    eng.tensor_add(r[:], p[:], r[:]).then_inc(sVE.h, 1)
                ph.on("vector", f_add)
                o_ev.append(tv)
                o_st[par] = dR2S[par].plan(16)
                ph.on("scalar", lambda eng, tv=tv, oc=oc, r=rt2[par], sm=dR2S[par]:
                      (eng.wait_ge(sVE.h, tv),
                       eng.dma_start(out=r_own[:, oc * T:(oc + 1) * T], in_=r[:]).then_inc(sm.h, 16)))

        def ln_phase(s1, src, xln):
            mb_sb = s1.enter_context(nc.sbuf_tensor(nm("mbsb"), [128, T], F32))
            rb_sb = s1.enter_context(nc.sbuf_tensor(nm("rbsb2"), [128, T], F32))
            with ExitStack() as s2:
                lnbuf = [s2.enter_context(nc.sbuf_tensor(nm(f"lnb{i}"), [128, T], F32)) for i in range(2)]
                lnsq = [s2.enter_context(nc.sbuf_tensor(nm(f"lnq{i}"), [128, T], F32)) for i in range(2)]
                lnxb = [s2.enter_context(nc.sbuf_tensor(nm(f"lnx{i}"), [128, T], BF16)) for i in range(2)]
                lnqb = [s2.enter_context(nc.sbuf_tensor(nm(f"lnz{i}"), [128, T], BF16)) for i in range(2)]
                pstat = s2.enter_context(nc.psum_tensor(nm("pstat"), [128, 2 * T], F32))
                pbc = s2.enter_context(nc.psum_tensor(nm("pbc"), [128, 2 * T], F32))
                return p_ln(src, xln, lnbuf, lnsq, lnxb, lnqb, pstat, pbc, mb_sb, rb_sb)

        def ffn_block(src, dst):
            with ExitStack() as s1:
                xln = s1.enter_context(nc.sbuf_tensor(nm("xln"), [128, HC * T], BF16))
                gbuf = [s1.enter_context(nc.sbuf_tensor(nm(f"gb{i}"), [128, HC * 128], BF16)) for i in range(2)]
                ubuf = [s1.enter_context(nc.sbuf_tensor(nm(f"ub{i}"), [128, HC * 128], BF16)) for i in range(2)]
                xg, cp_t = None, 0
                if next_phase():
                    xg, cp_t = ln_phase(s1, src, xln)
                if next_phase():
                    with ExitStack() as s2:
                        sscr = [s2.enter_context(nc.sbuf_tensor(nm(f"ss{i}"), [128, T], F32)) for i in range(2)]
                        itile = [s2.enter_context(nc.sbuf_tensor(nm(f"it{i}"), [128, T], BF16)) for i in range(2)]
                        pg = [s2.enter_context(nc.psum_tensor(nm(f"pg{i}"), [128, T], F32)) for i in range(2)]
                        pu = [s2.enter_context(nc.psum_tensor(nm(f"pu{i}"), [128, T], F32)) for i in range(2)]
                        p_a1(xln, gbuf, ubuf, sscr, itile, pg, pu, xg=xg, cp_t=cp_t)
                        ph.flush(block)
                        join_all()
            if next_phase():
                with ExitStack() as s2:
                    dbuf = [s2.enter_context(nc.sbuf_tensor(nm(f"db{i}"), [128, 22 * 512], BF16)) for i in range(4)]
                    ibuf = [s2.enter_context(nc.sbuf_tensor(nm(f"ib{i}"), [128, 4 * 512], BF16)) for i in range(4)]
                    rt = [s2.enter_context(nc.sbuf_tensor(nm(f"rt{i}"), [128, T], F32)) for i in range(4)]
                    pd = s2.enter_context(nc.psum_tensor(nm("pd"), [128, 4 * T], F32))
                    p_a2(src, dst, dbuf, ibuf, rt, pd)
                    ph.flush(block)
                    join_all()

        if next_phase():
            p_prologue()
            ph.flush(block)
            join_all()

        ffn_block(x_own, r_own)
        ffn_block(x_par, r_par)

        with ExitStack() as satt:
            K_sb = satt.enter_context(nc.sbuf_tensor("K_sb", [128, NKV * 2 * T], BF16))
            V_sb = satt.enter_context(nc.sbuf_tensor("V_sb", [128, 16 * T], BF16))
            for hf, rsrc in ((0, r_own), (1, r_par)):
                with ExitStack() as s1:
                    xln = s1.enter_context(nc.sbuf_tensor(nm("xln2"), [128, HC * T], BF16))
                    wslab = [s1.enter_context(nc.sbuf_tensor(nm(f"ws{i}"), [128, HC * 128], BF16)) for i in range(2)]
                    xg, cp_t = None, 0
                    if next_phase():
                        xg, cp_t = ln_phase(s1, rsrc, xln)
                    if next_phase():
                        with ExitStack() as s2:
                            vfm = [s2.enter_context(nc.sbuf_tensor(nm(f"vf{i}"), [128, T], BF16)) for i in range(2)]
                            qtile = [s2.enter_context(nc.sbuf_tensor(nm(f"qt{i}"), [128, T], BF16)) for i in range(2)]
                            pk = [s2.enter_context(nc.psum_tensor(nm(f"pk{i}"), [128, T], F32)) for i in range(2)]
                            pt = [s2.enter_context(nc.psum_tensor(nm(f"pt{i}"), [128, 128], BF16)) for i in range(2)]
                            p_b1(xln, hf, hf == 0, wslab, pk, vfm, pt, qtile, K_sb, V_sb, xg=xg, cp_t=cp_t)
                            ph.flush(block)
                            join_all()
            with ExitStack() as sctx:
                ctx_sb = sctx.enter_context(nc.sbuf_tensor("ctx_sb", [128, NQH * T], BF16))
                if next_phase():
                    with ExitStack() as s2:
                        qh = [s2.enter_context(nc.sbuf_tensor(nm(f"qh{i}"), [128, T], BF16)) for i in range(2)]
                        probs = [s2.enter_context(nc.sbuf_tensor(nm(f"pr{i}"), [128, T], BF16)) for i in range(4)]
                        rbsb = s2.enter_context(nc.sbuf_tensor(nm("rbsb"), [128, T], F32))
                        psc = [s2.enter_context(nc.psum_tensor(nm(f"psc{i}"), [128, T], F32)) for i in range(2)]
                        prow = s2.enter_context(nc.psum_tensor(nm("prow"), [128, T], F32))
                        pctx = s2.enter_context(nc.psum_tensor(nm("pctx"), [128, T], F32))
                        p_b2(K_sb, V_sb, ctx_sb, qh, probs, rbsb, psc, prow, pctx)
                        ph.flush(block)
                        join_all()
                if next_phase():
                    with ExitStack() as s2:
                        wslab = [s2.enter_context(nc.sbuf_tensor(nm(f"ws{i}"), [128, HC * 128], BF16)) for i in range(2)]
                        rt2 = [s2.enter_context(nc.sbuf_tensor(nm(f"r2{i}"), [128, T], F32)) for i in range(2)]
                        pw = [s2.enter_context(nc.psum_tensor(nm(f"pw{i}"), [128, T], F32)) for i in range(2)]
                        p_b3(ctx_sb, wslab, rt2, pw)
                        ph.flush(block)
                        join_all()

        ffn_block(r_own, out)

        if phase_no[0] > phase_limit:
            tgts = tuple((s, s.n) for s in all_sems)
            dRS.plan(16)
            ph.on("scalar", lambda eng, tg=tgts: (_w(eng, tg),
                  eng.dma_start(out=out, in_=r_own).then_inc(dRS.h, 16)))

        tgts = tuple((s, s.n) for s in all_sems)
        for e in ENGS:
            ph.on(e, lambda eng, tg=tgts: _w(eng, tg))
        ph.flush(block)

    return nc



def _prep_weights(inp):
    gate = np.asarray(inp["L0_ffn_gate_wq"])
    up = np.asarray(inp["L0_ffn_up_wq"])
    down = np.asarray(inp["L0_ffn_down_wq"])
    wq = np.asarray(inp["L0_attn_Wq"], dtype=np.float32)
    wk = np.asarray(inp["L0_attn_Wk"], dtype=np.float32)
    wv = np.asarray(inp["L0_attn_Wv"], dtype=np.float32)
    wo = np.asarray(inp["L0_attn_Wo"], dtype=np.float32)

    d = {}
    d["gate_w"] = np.ascontiguousarray(
        gate.reshape(IC, 128, HC, 128).transpose(0, 3, 2, 1)).reshape(IC, 128, HC * 128)
    d["up_w"] = np.ascontiguousarray(
        up.reshape(IC, 128, HC, 128).transpose(0, 3, 2, 1)).reshape(IC, 128, HC * 128)
    dT = np.ascontiguousarray(down.T)
    d["down_w"] = np.ascontiguousarray(
        dT.reshape(2, 43, 128, 8, 4, 128).transpose(3, 0, 2, 1, 4, 5)).reshape(8, 2, 128, 43 * 512)

    def proj(w, noc):
        return np.ascontiguousarray(
            w.reshape(noc, 128, HC, 128).transpose(0, 3, 2, 1)).reshape(noc, 128, HC * 128)

    d["wq_w"] = proj(wq, NQH)
    d["wk_w"] = proj(wk, NKV)
    d["wv_w"] = proj(wv, NKV)
    d["wo_w"] = proj(wo, HC)
    d["gate_sc"] = np.ascontiguousarray(np.asarray(inp["L0_ffn_gate_scale"], np.float32).reshape(IC, 128).T)
    d["up_sc"] = np.ascontiguousarray(np.asarray(inp["L0_ffn_up_scale"], np.float32).reshape(IC, 128).T)
    d["down_sc"] = np.ascontiguousarray(np.asarray(inp["L0_ffn_down_scale"], np.float32).reshape(HC, 128).T)
    d["bv_c"] = np.ascontiguousarray(np.asarray(inp["L0_attn_bv"], np.float32).reshape(NKV, 128).T)
    d["ident_in"] = np.eye(128, dtype=np.float32)
    return d


def _fm(x_half):
    return np.ascontiguousarray(x_half.reshape(T, HC, 128).transpose(2, 1, 0)).reshape(128, HC * T)


def kernel(**inputs):
    x = np.asarray(inputs["L0_attn_hidden_states"], dtype=np.float32)
    shared = _prep_weights(inputs)
    nc = build_nc()
    in_maps = []
    for c in range(8):
        b, hf = c // 2, c % 2
        m = {"x_own": _fm(x[b, hf * T:(hf + 1) * T]),
             "x_par": _fm(x[b, (1 - hf) * T:(2 - hf) * T])}
        m.update(shared)
        in_maps.append(m)
    res = run_bass_kernel_spmd(nc, in_maps, core_ids=list(range(8)))
    outp = np.empty((BATCH, S, H), dtype=np.float32)
    for c in range(8):
        b, hf = c // 2, c % 2
        o = np.asarray(res.results[c]["out"]).reshape(128, HC, T)
        outp[b, hf * T:(hf + 1) * T] = o.transpose(2, 1, 0).reshape(T, H)
    return outp


# version 08
# speedup vs baseline: 1.0881x; 1.0881x over previous
import numpy as np
from contextlib import ExitStack

import concourse.bass as bass
import concourse.mybir as mybir
from concourse.bass_utils import run_bass_kernel_spmd

F32 = mybir.dt.float32
BF16 = mybir.dt.bfloat16
I32 = mybir.dt.int32
AF = mybir.ActivationFunctionType
ALU = mybir.AluOpType

H = 4096
II = 11008
BATCH = 4
S = 2048
T = 1024
HC = H // 128
IC = II // 128
NQH = 32
NKV = 8
D = 128
GQ = NQH // NKV
NEG_STAB = -32.0
QSCALE = float(D) ** -0.5
EPS = 1e-5

ENGS = ("sync", "gpsimd", "tensor", "scalar", "vector")


class Sem:
    def __init__(self, handle):
        self.h = handle
        self.n = 0

    def plan(self, k):
        self.n += k
        return self.n


class Ph:
    def __init__(self):
        self.q = {e: [] for e in ENGS}

    def on(self, e, fn):
        self.q[e].append(fn)

    def flush(self, block):
        for e in ENGS:
            fns = self.q[e]
            if not fns:
                continue

            def runner(eng, fns=fns):
                for f in fns:
                    f(eng)

            getattr(block, e)(runner)
        self.q = {e: [] for e in ENGS}


def _w(eng, waits):
    for s, v in waits:
        if v > 0:
            eng.wait_ge(s.h, v)


def build_nc(phase_limit=99):
    nc = bass.Bass("TRN2", target_bir_lowering=False)

    x_own = nc.dram_tensor("x_own", [128, HC * T], F32, kind="ExternalInput").ap()
    x_par = nc.dram_tensor("x_par", [128, HC * T], F32, kind="ExternalInput").ap()
    gate_w = nc.dram_tensor("gate_w", [IC, 128, HC * 128], I32, kind="ExternalInput").ap()
    up_w = nc.dram_tensor("up_w", [IC, 128, HC * 128], I32, kind="ExternalInput").ap()
    down_w = nc.dram_tensor("down_w", [8, 2, 128, 43 * 512], I32, kind="ExternalInput").ap()
    wq_w = nc.dram_tensor("wq_w", [NQH, 128, HC * 128], F32, kind="ExternalInput").ap()
    wk_w = nc.dram_tensor("wk_w", [NKV, 128, HC * 128], F32, kind="ExternalInput").ap()
    wv_w = nc.dram_tensor("wv_w", [NKV, 128, HC * 128], F32, kind="ExternalInput").ap()
    wo_w = nc.dram_tensor("wo_w", [HC, 128, HC * 128], F32, kind="ExternalInput").ap()
    gate_sc = nc.dram_tensor("gate_sc", [128, IC], F32, kind="ExternalInput").ap()
    up_sc = nc.dram_tensor("up_sc", [128, IC], F32, kind="ExternalInput").ap()
    down_sc = nc.dram_tensor("down_sc", [128, HC], F32, kind="ExternalInput").ap()
    bv_c = nc.dram_tensor("bv_c", [128, NKV], F32, kind="ExternalInput").ap()
    ident_in = nc.dram_tensor("ident_in", [128, 128], F32, kind="ExternalInput").ap()

    out = nc.dram_tensor("out", [128, HC * T], F32, kind="ExternalOutput").ap()

    r_own = nc.dram_tensor("r_own", [128, HC * T], F32).ap()
    r_par = nc.dram_tensor("r_par", [128, HC * T], F32).ap()
    inter_h = nc.dram_tensor("inter_h", [128, IC * T], BF16).ap()
    q_h = nc.dram_tensor("q_h", [NQH, 128, T], BF16).ap()

    es = ExitStack()
    with es:
        ones_colf = es.enter_context(nc.sbuf_tensor("ones_colf", [128, 1], F32))
        ones_rowf = es.enter_context(nc.sbuf_tensor("ones_rowf", [1, 128], F32))
        ones_colb = es.enter_context(nc.sbuf_tensor("ones_colb", [128, 1], BF16))
        ones_sqb = es.enter_context(nc.sbuf_tensor("ones_sqb", [128, 128], BF16))
        neg32 = es.enter_context(nc.sbuf_tensor("neg32", [128, 1], F32))
        gsc = es.enter_context(nc.sbuf_tensor("gsc", [128, IC], F32))
        usc = es.enter_context(nc.sbuf_tensor("usc", [128, IC], F32))
        dsc = es.enter_context(nc.sbuf_tensor("dsc", [128, HC], F32))
        hdsc = es.enter_context(nc.sbuf_tensor("hdsc", [128, HC], F32))
        bvc = es.enter_context(nc.sbuf_tensor("bvc", [128, NKV], F32))
        identb = es.enter_context(nc.sbuf_tensor("identb", [128, 128], BF16))
        mrow = es.enter_context(nc.sbuf_tensor("mrow", [1, T], F32))
        vrow = es.enter_context(nc.sbuf_tensor("vrow", [1, T], F32))
        srow = es.enter_context(nc.sbuf_tensor("srow", [1, T], F32))
        rrow = es.enter_context(nc.sbuf_tensor("rrow", [1, T], F32))

        all_sems = []

        def mksem(name):
            s = Sem(es.enter_context(nc.semaphore(name)))
            all_sems.append(s)
            return s

        sPE = mksem("sPE")
        sSE = mksem("sSE")
        sVE = mksem("sVE")
        dPL = mksem("dPL")
        dID = mksem("dID")
        dLN = [mksem(f"dLN{i}") for i in range(2)]
        dWP = [mksem(f"dWP{i}") for i in range(2)]
        dIT = [mksem(f"dIT{i}") for i in range(2)]
        dDB = [mksem(f"dDB{i}") for i in range(4)]
        dIB = [mksem(f"dIB{i}") for i in range(4)]
        dRL = mksem("dRL")
        dRS = mksem("dRS")
        dWS = [mksem(f"dWS{i}") for i in range(2)]
        dQS = [mksem(f"dQS{i}") for i in range(2)]
        dQH = [mksem(f"dQH{i}") for i in range(2)]
        dR2L = [mksem(f"dR2L{i}") for i in range(2)]
        dR2S = [mksem(f"dR2S{i}") for i in range(2)]

        block = es.enter_context(nc.Block())
        ph = Ph()
        phase_no = [0]
        _uid = [0]

        def nm(base):
            _uid[0] += 1
            return f"{base}_{_uid[0]}"

        def join_all():
            tgts = tuple((s, s.n) for s in all_sems)
            for e in ENGS:
                ph.on(e, lambda eng, tg=tgts: _w(eng, tg))

        def next_phase():
            phase_no[0] += 1
            return phase_no[0] <= phase_limit

        def p_prologue():
            def f_ve(eng):
                eng.memset(ones_colf[:], 1.0)
                eng.memset(ones_rowf[:], 1.0)
                eng.memset(ones_colb[:], 1.0)
                eng.memset(ones_sqb[:], 1.0)
                eng.memset(neg32[:], NEG_STAB)
            ph.on("vector", f_ve)
            for dst, src in ((gsc, gate_sc), (usc, up_sc), (dsc, down_sc), (bvc, bv_c)):
                dPL.plan(16)
                ph.on("sync", lambda eng, d=dst, s=src: eng.dma_start(out=d[:], in_=s).then_inc(dPL.h, 16))
            dID.plan(16)
            ph.on("gpsimd", lambda eng: eng.dma_start(out=identb[:], in_=ident_in).then_inc(dID.h, 16))
            vall = dPL.n
            sVE.plan(1)
            def f_ve2(eng, vall=vall):
                eng.wait_ge(dPL.h, vall)
                eng.tensor_scalar_mul(hdsc[:], dsc[:], 0.5).then_inc(sVE.h, 1)
            ph.on("vector", f_ve2)

        def p_ln(src, xln, lnbuf, lnsq, lnxb, lnqb, pstat, pbc, mb_sb, rb_sb):
            stat_pe = []
            cast_ve = []
            sq_se = []
            for hc in range(HC):
                par = hc % 2
                wait = []
                if hc >= 2:
                    wait.append((sVE, cast_ve[hc - 2]))
                    wait.append((sSE, sq_se[hc - 2]))
                v = dLN[par].plan(16)
                ph.on("sync", lambda eng, w=tuple(wait), s=src, hc=hc, b=lnbuf[par], sm=dLN[par]:
                      (_w(eng, w), eng.dma_start(out=b[:], in_=s[:, hc * T:(hc + 1) * T]).then_inc(sm.h, 16)))
                waitc = [(dLN[par], v)]
                if hc >= 2:
                    waitc.append((sPE, stat_pe[hc - 2]))
                tv = sVE.plan(1)
                ph.on("vector", lambda eng, w=tuple(waitc), b=lnbuf[par], xb=lnxb[par]:
                      (_w(eng, w), eng.tensor_copy(xb[:], b[:]).then_inc(sVE.h, 1)))
                cast_ve.append(tv)
                ts = sSE.plan(1)
                ph.on("scalar", lambda eng, w=tuple(waitc), b=lnbuf[par], qb=lnqb[par]:
                      (_w(eng, w), eng.activation(qb[:], b[:], AF.Square).then_inc(sSE.h, 1)))
                sq_se.append(ts)
                tp = sPE.plan(1)
                def f_pe(eng, tv=tv, ts=ts, hc=hc, xb=lnxb[par], qb=lnqb[par]):
                    eng.wait_ge(sVE.h, tv)
                    eng.wait_ge(sSE.h, ts)
                    st = hc == 0
                    sp = hc == HC - 1
                    for half in range(2):
                        eng.matmul(pstat[:, half * 512:(half + 1) * 512], ones_sqb[:],
                                   xb[:, half * 512:(half + 1) * 512], start=st, stop=sp)
                    for half in range(2):
                        i = eng.matmul(pstat[:, T + half * 512:T + (half + 1) * 512], ones_sqb[:],
                                       qb[:, half * 512:(half + 1) * 512], start=st, stop=sp)
                    i.then_inc(sPE.h, 1)
                ph.on("tensor", f_pe)
                stat_pe.append(tp)
            tp_all = sPE.n
            v1 = sVE.plan(1); sVE.plan(1); v3 = sVE.plan(1); v4 = sVE.plan(1); v5 = sVE.plan(1)
            def f_fin_ve(eng, tp_all=tp_all, v1=v1, v3=v3, v4=v4):
                eng.wait_ge(sPE.h, tp_all)
                eng.tensor_scalar_mul(mrow[:], pstat[:1, 0:T], 1.0 / H).then_inc(sVE.h, 1)
                eng.tensor_scalar_mul(vrow[:], pstat[:1, T:2 * T], 1.0 / H).then_inc(sVE.h, 1)
                eng.wait_ge(sVE.h, v1)
                eng.tensor_mul(srow[:], mrow[:], mrow[:]).then_inc(sVE.h, 1)
                eng.wait_ge(sVE.h, v3)
                eng.tensor_scalar(srow[:], srow[:], -1.0, EPS, ALU.mult, ALU.add).then_inc(sVE.h, 1)
                eng.wait_ge(sVE.h, v4)
                eng.tensor_add(vrow[:], vrow[:], srow[:]).then_inc(sVE.h, 1)
            ph.on("vector", f_fin_ve)
            ts1 = sSE.plan(1)
            ph.on("scalar", lambda eng, v5=v5: (eng.wait_ge(sVE.h, v5),
                  eng.activation(srow[:], vrow[:], AF.Sqrt).then_inc(sSE.h, 1)))
            tv6 = sVE.plan(1)
            ph.on("vector", lambda eng, ts1=ts1: (eng.wait_ge(sSE.h, ts1),
                  eng.reciprocal(rrow[:], srow[:]).then_inc(sVE.h, 1)))
            tp_bc = sPE.plan(1)
            def f_bc(eng, tv6=tv6):
                eng.wait_ge(sVE.h, tv6)
                for half in range(2):
                    eng.matmul(pbc[:, half * 512:(half + 1) * 512], ones_rowf[:],
                               mrow[:, half * 512:(half + 1) * 512], start=True, stop=True)
                for half in range(2):
                    i = eng.matmul(pbc[:, T + half * 512:T + (half + 1) * 512], ones_rowf[:],
                                   rrow[:, half * 512:(half + 1) * 512], start=True, stop=True)
                i.then_inc(sPE.h, 1)
            ph.on("tensor", f_bc)
            sSE.plan(1)
            cp_t = sSE.plan(1)
            def f_bcp(eng, tp_bc=tp_bc):
                eng.wait_ge(sPE.h, tp_bc)
                eng.copy(mb_sb[:], pbc[:, 0:T]).then_inc(sSE.h, 1)
                eng.copy(rb_sb[:], pbc[:, T:2 * T]).then_inc(sSE.h, 1)
            ph.on("scalar", f_bcp)
            app_ve = []
            for hc in range(HC):
                par = hc % 2
                wait = [(sPE, tp_all)] if hc < 2 else [(sVE, app_ve[hc - 2])]
                v = dLN[par].plan(16)
                ph.on("sync", lambda eng, w=tuple(wait), s=src, hc=hc, b=lnbuf[par], sm=dLN[par]:
                      (_w(eng, w), eng.dma_start(out=b[:], in_=s[:, hc * T:(hc + 1) * T]).then_inc(sm.h, 16)))
                va = sVE.plan(1)
                vb = sVE.plan(1)
                def f_app(eng, v=v, hc=hc, b=lnbuf[par], q=lnsq[par], cp_t=cp_t, va=va, sm=dLN[par]):
                    eng.wait_ge(sm.h, v)
                    eng.wait_ge(sSE.h, cp_t)
                    eng.tensor_sub(q[:], b[:], mb_sb[:]).then_inc(sVE.h, 1)
                    eng.wait_ge(sVE.h, va)
                    eng.tensor_tensor(xln[:, hc * T:(hc + 1) * T], q[:], rb_sb[:], ALU.mult).then_inc(sVE.h, 1)
                ph.on("vector", f_app)
                app_ve.append(vb)
            return app_ve, cp_t

        def p_a1(xln, gbuf, ubuf, sscr, itile, pg, pu, xg=None, cp_t=0):
            pe_done, ve_done, se_done, st_done = [], [], [], []
            for ic in range(IC):
                par = ic % 2
                waitw = [(sPE, pe_done[ic - 2])] if ic >= 2 else []
                dWP[par].plan(16)
                vu = dWP[par].plan(16)
                def f_dma(eng, w=tuple(waitw), ic=ic, g=gbuf[par], u=ubuf[par], sm=dWP[par]):
                    _w(eng, w)
                    eng.dma_start(out=g[:], in_=gate_w[ic]).then_inc(sm.h, 16)
                    eng.dma_start(out=u[:], in_=up_w[ic]).then_inc(sm.h, 16)
                ph.on("gpsimd", f_dma)
                waitp = [(dWP[par], vu)]
                if ic >= 2:
                    waitp.append((sSE, se_done[ic - 2]))
                    waitp.append((sVE, ve_done[ic - 2]))
                elif cp_t:
                    waitp.append((sSE, cp_t))
                gates = xg if (xg is not None and ic < 2) else None
                tg = sPE.plan(1)
                tu = sPE.plan(1)
                def f_mm(eng, w=tuple(waitp), g=gbuf[par], u=ubuf[par], pg_=pg[par], pu_=pu[par], gates=gates):
                    _w(eng, w)
                    for hc in range(HC):
                        st = hc == 0
                        sp = hc == HC - 1
                        if gates is not None:
                            eng.wait_ge(sVE.h, gates[hc])
                        lhs = g[:, hc * 128:(hc + 1) * 128]
                        eng.matmul(pg_[:, 0:512], lhs, xln[:, hc * T:hc * T + 512], start=st, stop=sp)
                        i = eng.matmul(pg_[:, 512:1024], lhs, xln[:, hc * T + 512:(hc + 1) * T], start=st, stop=sp)
                    i.then_inc(sPE.h, 1)
                    for hc in range(HC):
                        st = hc == 0
                        sp = hc == HC - 1
                        lhs = u[:, hc * 128:(hc + 1) * 128]
                        eng.matmul(pu_[:, 0:512], lhs, xln[:, hc * T:hc * T + 512], start=st, stop=sp)
                        i = eng.matmul(pu_[:, 512:1024], lhs, xln[:, hc * T + 512:(hc + 1) * T], start=st, stop=sp)
                    i.then_inc(sPE.h, 1)
                ph.on("tensor", f_mm)
                pe_done.append(tu)
                ts = sSE.plan(1)
                def f_silu(eng, tg=tg, ic=ic, pg_=pg[par], sc=sscr[par]):
                    eng.wait_ge(sPE.h, tg)
                    eng.activation(sc[:], pg_[:], AF.Silu, scale=gsc[:, ic:ic + 1]).then_inc(sSE.h, 1)
                ph.on("scalar", f_silu)
                se_done.append(ts)
                waitv = [(sPE, tu), (sSE, ts)]
                if ic >= 2:
                    waitv.append((dIT[par], st_done[ic - 2]))
                tv = sVE.plan(1)
                def f_stt(eng, w=tuple(waitv), ic=ic, pu_=pu[par], sc=sscr[par], it=itile[par]):
                    _w(eng, w)
                    eng.scalar_tensor_tensor(it[:], pu_[:], usc[:, ic:ic + 1], sc[:], ALU.mult, ALU.mult).then_inc(sVE.h, 1)
                ph.on("vector", f_stt)
                ve_done.append(tv)
                vs = dIT[par].plan(16)
                ph.on("scalar", lambda eng, tv=tv, ic=ic, it=itile[par], sm=dIT[par]:
                      (eng.wait_ge(sVE.h, tv),
                       eng.dma_start(out=inter_h[:, ic * T:(ic + 1) * T], in_=it[:]).then_inc(sm.h, 16)))
                st_done.append(vs)

        QSTART = (0, 22, 43, 65)
        QLEN = (22, 21, 22, 21)

        def q_of(ic):
            return 0 if ic < 22 else 1 if ic < 43 else 2 if ic < 65 else 3

        def p_a2(r_src, r_dst, dbuf, ibuf, rt, pd):
            NG = 22
            prev_store = 0
            prev_ve = 0
            last_ig = [0, 0, 0, 0]
            q_first = [None] * 4
            q_last = [0] * 4
            for g in range(NG):
                n = 4 if g < 21 else 2
                for j in range(n):
                    q = q_of(g * 4 + j)
                    if q_first[q] is None:
                        q_first[q] = g
                    q_last[q] = g
            dq_done = [0, 0, 0, 0]
            for s8 in range(8):
                wdq = []
                for q in range(4):
                    hf = 0 if q < 2 else 1
                    off = QSTART[q] - (0 if q < 2 else 43)
                    wait = [(sPE, dq_done[q])]
                    v = dDB[q].plan(16)
                    ph.on("gpsimd", lambda eng, w=tuple(wait), s8=s8, q=q, hf=hf, off=off, sm=dDB[q]:
                          (_w(eng, w), eng.dma_start(out=dbuf[q][:, 0:QLEN[q] * 512], in_=down_w[s8, hf][:, off * 512:(off + QLEN[q]) * 512]).then_inc(sm.h, 16)))
                    wdq.append(v)
                wait = [(dRS, prev_store)] if s8 > 0 else []
                for oi in range(4):
                    dRL.plan(16)
                    ph.on("scalar", lambda eng, w=tuple(wait), s8=s8, oi=oi, r=rt[oi]:
                          (_w(eng, w), eng.dma_start(out=r[:], in_=r_src[:, (s8 * 4 + oi) * T:(s8 * 4 + oi + 1) * T]).then_inc(dRL.h, 16)))
                rl_all = dRL.n
                ig_done = []
                for g in range(NG):
                    par = g % 4
                    n = 4 if g < 21 else 2
                    wait = [(sPE, ig_done[g - 4])] if g >= 4 else [(sPE, last_ig[g])]
                    v = dIB[par].plan(16)
                    ph.on("sync", lambda eng, w=tuple(wait), g=g, n=n, b=ibuf[par], sm=dIB[par]:
                          (_w(eng, w), eng.dma_start(out=b[:, 0:n * T], in_=inter_h[:, g * 4 * T:(g * 4 + n) * T]).then_inc(sm.h, 16)))
                    tp = sPE.plan(1)
                    def f_mm(eng, v=v, g=g, n=n, b=ibuf[par], wdq=tuple(wdq), pv=prev_ve, par=par,
                             qf=tuple(q_first)):
                        eng.wait_ge(dIB[par].h, v)
                        if g == 0 and pv:
                            eng.wait_ge(sVE.h, pv)
                        for q in range(4):
                            if qf[q] == g:
                                eng.wait_ge(dDB[q].h, wdq[q])
                        for j in range(n):
                            ic = g * 4 + j
                            q = q_of(ic)
                            off = ic - QSTART[q]
                            st = ic == 0
                            sp = ic == IC - 1
                            for oi in range(4):
                                lhs = dbuf[q][:, (off * 4 + oi) * 128:(off * 4 + oi + 1) * 128]
                                eng.matmul(pd[:, oi * T:oi * T + 512], lhs, b[:, j * T:j * T + 512], start=st, stop=sp)
                                i = eng.matmul(pd[:, oi * T + 512:(oi + 1) * T], lhs, b[:, j * T + 512:(j + 1) * T], start=st, stop=sp)
                        i.then_inc(sPE.h, 1)
                    ph.on("tensor", f_mm)
                    ig_done.append(tp)
                last_ig = [ig_done[20], ig_done[21], ig_done[18], ig_done[19]]
                dq_done = [ig_done[q_last[q]] for q in range(4)]
                tall = ig_done[-1]
                tv = sVE.plan(1)
                def f_ev(eng, tall=tall, rl_all=rl_all, s8=s8):
                    eng.wait_ge(sPE.h, tall)
                    eng.wait_ge(dRL.h, rl_all)
                    for oi in range(4):
                        i = eng.scalar_tensor_tensor(rt[oi][:], pd[:, oi * T:(oi + 1) * T],
                                                     hdsc[:, s8 * 4 + oi:s8 * 4 + oi + 1], rt[oi][:],
                                                     ALU.mult, ALU.add)
                    i.then_inc(sVE.h, 1)
                ph.on("vector", f_ev)
                prev_ve = tv
                for oi in range(4):
                    dRS.plan(16)
                    ph.on("scalar", lambda eng, tv=tv, s8=s8, oi=oi, r=rt[oi]:
                          (eng.wait_ge(sVE.h, tv),
                           eng.dma_start(out=r_dst[:, (s8 * 4 + oi) * T:(s8 * 4 + oi + 1) * T], in_=r[:]).then_inc(dRS.h, 16)))
                prev_store = dRS.n

        def proj_loop(xln, w_dram, nout, wslab, pk, slab_free, psum_free, on_out, xg=None, cp_t=0):
            mm_done = []
            ev_done = []
            for j in range(nout):
                par = j % 2
                wait = [slab_free(j)] if j < 2 else [(sPE, mm_done[j - 2])]
                v = dWS[par].plan(16)
                ph.on("gpsimd", lambda eng, w=tuple(x for x in wait if x), j=j, b=wslab[par], sm=dWS[par]:
                      (_w(eng, w), eng.dma_start(out=b[:], in_=w_dram[j]).then_inc(sm.h, 16)))
                waitp = [(dWS[par], v)]
                ep = ev_done[j - 2] if j >= 2 else psum_free(j)
                if ep:
                    waitp.append(ep)
                if j < 2 and cp_t:
                    waitp.append((sSE, cp_t))
                gates = xg if (xg is not None and j < 2) else None
                tp = sPE.plan(1)
                def f_mm(eng, w=tuple(waitp), b=wslab[par], p=pk[par], gates=gates):
                    _w(eng, w)
                    for hc in range(HC):
                        st = hc == 0
                        sp = hc == HC - 1
                        if gates is not None:
                            eng.wait_ge(sVE.h, gates[hc])
                        lhs = b[:, hc * 128:(hc + 1) * 128]
                        eng.matmul(p[:, 0:512], lhs, xln[:, hc * T:hc * T + 512], start=st, stop=sp)
                        i = eng.matmul(p[:, 512:1024], lhs, xln[:, hc * T + 512:(hc + 1) * T], start=st, stop=sp)
                    i.then_inc(sPE.h, 1)
                ph.on("tensor", f_mm)
                mm_done.append(tp)
                ev_done.append(on_out(j, tp))
            return mm_done, ev_done

        def p_b1(xln, hf, do_q, wslab, pk, vfm, pt, qtile, K_sb, V_sb, xg=None, cp_t=0):
            def k_out(ko, tp):
                ts = sSE.plan(1)
                ph.on("scalar", lambda eng, tp=tp, ko=ko, p=pk[ko % 2]:
                      (eng.wait_ge(sPE.h, tp),
                       eng.copy(K_sb[:, (ko * 2 + hf) * T:(ko * 2 + hf + 1) * T], p[:]).then_inc(sSE.h, 1)))
                return (sSE, ts)
            k_mm, k_ev = proj_loop(xln, wk_w, NKV, wslab, pk,
                                   lambda j: None, lambda j: None, k_out, xg=xg, cp_t=cp_t)

            def v_out(vo, tp):
                ts = sSE.plan(1)
                def f_ev(eng, tp=tp, vo=vo, p=pk[vo % 2], f=vfm[vo % 2]):
                    eng.wait_ge(sPE.h, tp)
                    eng.activation(f[:], p[:], AF.Identity, bias=bvc[:, vo:vo + 1]).then_inc(sSE.h, 1)
                ph.on("scalar", f_ev)
                last = None
                tr = []
                for tt in range(8):
                    waitt = [(sSE, ts)]
                    if tt >= 2:
                        waitt.append((sSE, tr[tt - 2]))
                    tpx = sPE.plan(1)
                    def f_tr(eng, w=tuple(waitt), tt=tt, f=vfm[vo % 2], p=pt[tt % 2]):
                        _w(eng, w)
                        eng.transpose(p[:], f[:, tt * 128:(tt + 1) * 128], identb[:]).then_inc(sPE.h, 1)
                    ph.on("tensor", f_tr)
                    tse = sSE.plan(1)
                    def f_cp(eng, tpx=tpx, tt=tt, vo=vo, p=pt[tt % 2]):
                        eng.wait_ge(sPE.h, tpx)
                        eng.copy(V_sb[:, (hf * 8 + tt) * T + vo * 128:(hf * 8 + tt) * T + (vo + 1) * 128], p[:]).then_inc(sSE.h, 1)
                    ph.on("scalar", f_cp)
                    tr.append(tse)
                    last = tse
                return (sSE, last)
            v_mm, v_ev = proj_loop(xln, wv_w, NKV, wslab, pk,
                                   lambda j: (sPE, k_mm[-1]),
                                   lambda j: k_ev[NKV - 2 + (j % 2)], v_out)

            if do_q:
                qst = [0, 0]
                def q_out(qo, tp):
                    par = qo % 2
                    waitv = [(sPE, tp)]
                    if qo >= 2:
                        waitv.append((dQS[par], qst[par]))
                    tv = sVE.plan(1)
                    def f_sc(eng, w=tuple(waitv), p=pk[par], qt=qtile[par]):
                        _w(eng, w)
                        eng.tensor_scalar_mul(qt[:], p[:], QSCALE).then_inc(sVE.h, 1)
                    ph.on("vector", f_sc)
                    qst[par] = dQS[par].plan(16)
                    ph.on("scalar", lambda eng, tv=tv, qo=qo, qt=qtile[par], sm=dQS[par]:
                          (eng.wait_ge(sVE.h, tv),
                           eng.dma_start(out=q_h[qo], in_=qt[:]).then_inc(sm.h, 16)))
                    return (sVE, tv)
                proj_loop(xln, wq_w, NQH, wslab, pk,
                          lambda j: (sPE, v_mm[-1]),
                          lambda j: v_ev[NKV - 2 + (j % 2)], q_out)

        def p_b2(K_sb, V_sb, ctx_sb, qh, probs, rbsb, psc, prow, pctx, craw):
            qld = {}
            for h in range(2):
                v = dQH[h % 2].plan(16)
                ph.on("sync", lambda eng, h=h, b=qh[h % 2], sm=dQH[h % 2]:
                      eng.dma_start(out=b[:], in_=q_h[h]).then_inc(sm.h, 16))
                qld[h] = v
            prev_exp = [0] * 16
            prev_su = [0] * 16
            prev_rb_copy = 0
            prev_craw = 0
            prev_norm = 0
            prev_norm1 = 0
            prev_norm_last = 0
            pend_rb = None
            for h in range(NQH):
                g = h // GQ
                sc_t = [0] * 16
                su_t = [0] * 16
                exp_t = [0] * 16

                def plan_scores(t, h=h, g=g, prev_exp=prev_exp, prev_su=prev_su):
                    waitp = []
                    if t == 0:
                        waitp.append((dQH[h % 2], qld[h]))
                    pe_free = prev_exp[14 + t] if t < 2 else exp_t[t - 2]
                    if pe_free:
                        waitp.append((sSE, pe_free))
                    tp = sPE.plan(1)
                    def f_sc(eng, w=tuple(waitp), h=h, g=g, t=t, p=psc[t % 2]):
                        _w(eng, w)
                        base = (g * 2 + (t // 8)) * T + (t % 8) * 128
                        lhs = K_sb[:, base:base + 128]
                        eng.matmul(p[:, 0:512], lhs, qh[h % 2][:, 0:512], start=True, stop=True)
                        eng.matmul(p[:, 512:1024], lhs, qh[h % 2][:, 512:1024], start=True, stop=True).then_inc(sPE.h, 1)
                    ph.on("tensor", f_sc)
                    sc_t[t] = tp
                    waite = [(sPE, tp)]
                    pw = prev_su[12 + t] if t < 4 else su_t[t - 4]
                    if pw:
                        waite.append((sPE, pw))
                    ts = sSE.plan(1)
                    def f_exp(eng, w=tuple(waite), t=t, p=psc[t % 2], pr=probs[t % 4]):
                        _w(eng, w)
                        eng.activation(pr[:], p[:], AF.Exp, bias=neg32[:]).then_inc(sSE.h, 1)
                    ph.on("scalar", f_exp)
                    exp_t[t] = ts

                def plan_sum(t, h=h, g=g, prev_craw=prev_craw, prev_rb_copy=prev_rb_copy):
                    waitq = [(sSE, exp_t[t])]
                    if t == 0:
                        if prev_craw:
                            waitq.append((sVE, prev_craw))
                        if prev_rb_copy:
                            waitq.append((sSE, prev_rb_copy))
                    tq = sPE.plan(1)
                    def f_sum(eng, w=tuple(waitq), t=t, g=g, pr=probs[t % 4]):
                        _w(eng, w)
                        st = t == 0
                        sp = t == 15
                        eng.matmul(prow[:, 0:512], ones_sqb[:], pr[:, 0:512], start=st, stop=sp)
                        eng.matmul(prow[:, 512:1024], ones_sqb[:], pr[:, 512:1024], start=st, stop=sp)
                        lhs = V_sb[:, t * T + g * 128:t * T + (g + 1) * 128]
                        eng.matmul(pctx[:, 0:512], lhs, pr[:, 0:512], start=st, stop=sp)
                        eng.matmul(pctx[:, 512:1024], lhs, pr[:, 512:1024], start=st, stop=sp).then_inc(sPE.h, 1)
                    ph.on("tensor", f_sum)
                    su_t[t] = tq

                plan_scores(0)
                plan_scores(1)
                if pend_rb is not None:
                    pend_rb()
                    pend_rb = None
                for t in range(16):
                    if t + 2 < 16:
                        plan_scores(t + 2)
                    plan_sum(t)
                if h + 2 < NQH:
                    v = dQH[h % 2].plan(16)
                    ph.on("sync", lambda eng, w=sc_t[15], h2=h + 2, b=qh[h % 2], sm=dQH[h % 2]:
                          (eng.wait_ge(sPE.h, w), eng.dma_start(out=b[:], in_=q_h[h2]).then_inc(sm.h, 16)))
                    qld[h + 2] = v
                tv1 = sVE.plan(1)
                tvc = sVE.plan(1)
                def f_ve_tail(eng, tsum=su_t[15], h=h, pnorm=prev_norm):
                    eng.wait_ge(sPE.h, tsum)
                    eng.reciprocal(rrow[:], prow[:1, 0:T]).then_inc(sVE.h, 1)
                    if pnorm:
                        eng.wait_ge(sVE.h, pnorm)
                    eng.tensor_copy(craw[h % 2][:], pctx[:]).then_inc(sVE.h, 1)
                ph.on("vector", f_ve_tail)
                tpb = sPE.plan(1)
                tse = sSE.plan(1)
                tv2 = sVE.plan(1)
                def mk_rb(tv1=tv1, tvc=tvc, tpb=tpb, tse=tse, tv2=tv2, h=h, prc=prev_rb_copy, pcv=prev_norm1):
                    def f_rb(eng):
                        eng.wait_ge(sVE.h, tv1)
                        if prc:
                            eng.wait_ge(sSE.h, prc)
                        eng.matmul(prow[:, 0:512], ones_rowf[:], rrow[:, 0:512], start=True, stop=True)
                        eng.matmul(prow[:, 512:1024], ones_rowf[:], rrow[:, 512:1024], start=True, stop=True).then_inc(sPE.h, 1)
                    ph.on("tensor", f_rb)
                    def f_cp(eng):
                        eng.wait_ge(sPE.h, tpb)
                        if pcv:
                            eng.wait_ge(sVE.h, pcv)
                        eng.copy(rbsb[:], prow[:]).then_inc(sSE.h, 1)
                    ph.on("scalar", f_cp)
                    def f_norm(eng):
                        eng.wait_ge(sSE.h, tse)
                        eng.tensor_tensor(ctx_sb[:, h * T:(h + 1) * T], craw[h % 2][:], rbsb[:], ALU.mult).then_inc(sVE.h, 1)
                    ph.on("vector", f_norm)
                pend_rb = mk_rb()
                prev_exp = list(exp_t)
                prev_su = list(su_t)
                prev_norm1 = tv2
                prev_norm = prev_norm_last
                prev_norm_last = tv2
                prev_craw = tvc
                prev_rb_copy = tse
            if pend_rb is not None:
                pend_rb()

        def p_b3(ctx_sb, wslab, rt2, pw):
            o_mm, o_ev = [], []
            o_st = [0, 0]
            for oc in range(HC):
                par = oc % 2
                wait = [(sPE, o_mm[oc - 2])] if oc >= 2 else []
                v = dWS[par].plan(16)
                ph.on("gpsimd", lambda eng, w=tuple(wait), oc=oc, b=wslab[par], sm=dWS[par]:
                      (_w(eng, w), eng.dma_start(out=b[:], in_=wo_w[oc]).then_inc(sm.h, 16)))
                waitr = [(dR2S[par], o_st[par])] if oc >= 2 else []
                vr = dR2L[par].plan(16)
                ph.on("sync", lambda eng, w=tuple(waitr), oc=oc, r=rt2[par], sm=dR2L[par]:
                      (_w(eng, w), eng.dma_start(out=r[:], in_=r_own[:, oc * T:(oc + 1) * T]).then_inc(sm.h, 16)))
                waitp = [(dWS[par], v)]
                if oc >= 2:
                    waitp.append((sVE, o_ev[oc - 2]))
                tp = sPE.plan(1)
                def f_mm(eng, w=tuple(waitp), b=wslab[par], p=pw[par]):
                    _w(eng, w)
                    for hc in range(HC):
                        st = hc == 0
                        sp = hc == HC - 1
                        lhs = b[:, hc * 128:(hc + 1) * 128]
                        eng.matmul(p[:, 0:512], lhs, ctx_sb[:, hc * T:hc * T + 512], start=st, stop=sp)
                        i = eng.matmul(p[:, 512:1024], lhs, ctx_sb[:, hc * T + 512:(hc + 1) * T], start=st, stop=sp)
                    i.then_inc(sPE.h, 1)
                ph.on("tensor", f_mm)
                o_mm.append(tp)
                tv = sVE.plan(1)
                def f_add(eng, tp=tp, vr=vr, p=pw[par], r=rt2[par], sm=dR2L[par]):
                    eng.wait_ge(sPE.h, tp)
                    eng.wait_ge(sm.h, vr)
                    eng.tensor_add(r[:], p[:], r[:]).then_inc(sVE.h, 1)
                ph.on("vector", f_add)
                o_ev.append(tv)
                o_st[par] = dR2S[par].plan(16)
                ph.on("scalar", lambda eng, tv=tv, oc=oc, r=rt2[par], sm=dR2S[par]:
                      (eng.wait_ge(sVE.h, tv),
                       eng.dma_start(out=r_own[:, oc * T:(oc + 1) * T], in_=r[:]).then_inc(sm.h, 16)))

        def ln_phase(s1, src, xln):
            mb_sb = s1.enter_context(nc.sbuf_tensor(nm("mbsb"), [128, T], F32))
            rb_sb = s1.enter_context(nc.sbuf_tensor(nm("rbsb2"), [128, T], F32))
            with ExitStack() as s2:
                lnbuf = [s2.enter_context(nc.sbuf_tensor(nm(f"lnb{i}"), [128, T], F32)) for i in range(2)]
                lnsq = [s2.enter_context(nc.sbuf_tensor(nm(f"lnq{i}"), [128, T], F32)) for i in range(2)]
                lnxb = [s2.enter_context(nc.sbuf_tensor(nm(f"lnx{i}"), [128, T], BF16)) for i in range(2)]
                lnqb = [s2.enter_context(nc.sbuf_tensor(nm(f"lnz{i}"), [128, T], BF16)) for i in range(2)]
                pstat = s2.enter_context(nc.psum_tensor(nm("pstat"), [128, 2 * T], F32))
                pbc = s2.enter_context(nc.psum_tensor(nm("pbc"), [128, 2 * T], F32))
                return p_ln(src, xln, lnbuf, lnsq, lnxb, lnqb, pstat, pbc, mb_sb, rb_sb)

        def ffn_block(src, dst):
            with ExitStack() as s1:
                xln = s1.enter_context(nc.sbuf_tensor(nm("xln"), [128, HC * T], BF16))
                gbuf = [s1.enter_context(nc.sbuf_tensor(nm(f"gb{i}"), [128, HC * 128], BF16)) for i in range(2)]
                ubuf = [s1.enter_context(nc.sbuf_tensor(nm(f"ub{i}"), [128, HC * 128], BF16)) for i in range(2)]
                xg, cp_t = None, 0
                if next_phase():
                    xg, cp_t = ln_phase(s1, src, xln)
                if next_phase():
                    with ExitStack() as s2:
                        sscr = [s2.enter_context(nc.sbuf_tensor(nm(f"ss{i}"), [128, T], F32)) for i in range(2)]
                        itile = [s2.enter_context(nc.sbuf_tensor(nm(f"it{i}"), [128, T], BF16)) for i in range(2)]
                        pg = [s2.enter_context(nc.psum_tensor(nm(f"pg{i}"), [128, T], F32)) for i in range(2)]
                        pu = [s2.enter_context(nc.psum_tensor(nm(f"pu{i}"), [128, T], F32)) for i in range(2)]
                        p_a1(xln, gbuf, ubuf, sscr, itile, pg, pu, xg=xg, cp_t=cp_t)
                        ph.flush(block)
                        join_all()
            if next_phase():
                with ExitStack() as s2:
                    dbuf = [s2.enter_context(nc.sbuf_tensor(nm(f"db{i}"), [128, 22 * 512], BF16)) for i in range(4)]
                    ibuf = [s2.enter_context(nc.sbuf_tensor(nm(f"ib{i}"), [128, 4 * T], BF16)) for i in range(4)]
                    rt = [s2.enter_context(nc.sbuf_tensor(nm(f"rt{i}"), [128, T], F32)) for i in range(4)]
                    pd = s2.enter_context(nc.psum_tensor(nm("pd"), [128, 4 * T], F32))
                    p_a2(src, dst, dbuf, ibuf, rt, pd)
                    ph.flush(block)
                    join_all()

        if next_phase():
            p_prologue()
            ph.flush(block)
            join_all()

        ffn_block(x_own, r_own)
        ffn_block(x_par, r_par)

        with ExitStack() as satt:
            K_sb = satt.enter_context(nc.sbuf_tensor("K_sb", [128, NKV * 2 * T], BF16))
            V_sb = satt.enter_context(nc.sbuf_tensor("V_sb", [128, 16 * T], BF16))
            for hf, rsrc in ((0, r_own), (1, r_par)):
                with ExitStack() as s1:
                    xln = s1.enter_context(nc.sbuf_tensor(nm("xln2"), [128, HC * T], BF16))
                    wslab = [s1.enter_context(nc.sbuf_tensor(nm(f"ws{i}"), [128, HC * 128], BF16)) for i in range(2)]
                    xg, cp_t = None, 0
                    if next_phase():
                        xg, cp_t = ln_phase(s1, rsrc, xln)
                    if next_phase():
                        with ExitStack() as s2:
                            vfm = [s2.enter_context(nc.sbuf_tensor(nm(f"vf{i}"), [128, T], BF16)) for i in range(2)]
                            qtile = [s2.enter_context(nc.sbuf_tensor(nm(f"qt{i}"), [128, T], BF16)) for i in range(2)]
                            pk = [s2.enter_context(nc.psum_tensor(nm(f"pk{i}"), [128, T], F32)) for i in range(2)]
                            pt = [s2.enter_context(nc.psum_tensor(nm(f"pt{i}"), [128, 128], BF16)) for i in range(2)]
                            p_b1(xln, hf, hf == 0, wslab, pk, vfm, pt, qtile, K_sb, V_sb, xg=xg, cp_t=cp_t)
                            ph.flush(block)
                            join_all()
            with ExitStack() as sctx:
                ctx_sb = sctx.enter_context(nc.sbuf_tensor("ctx_sb", [128, NQH * T], BF16))
                if next_phase():
                    with ExitStack() as s2:
                        qh = [s2.enter_context(nc.sbuf_tensor(nm(f"qh{i}"), [128, T], BF16)) for i in range(2)]
                        probs = [s2.enter_context(nc.sbuf_tensor(nm(f"pr{i}"), [128, T], BF16)) for i in range(4)]
                        craw = [s2.enter_context(nc.sbuf_tensor(nm(f"cr{i}"), [128, T], BF16)) for i in range(2)]
                        rbsb = s2.enter_context(nc.sbuf_tensor(nm("rbsb"), [128, T], F32))
                        psc = [s2.enter_context(nc.psum_tensor(nm(f"psc{i}"), [128, T], F32)) for i in range(2)]
                        prow = s2.enter_context(nc.psum_tensor(nm("prow"), [128, T], F32))
                        pctx = s2.enter_context(nc.psum_tensor(nm("pctx"), [128, T], F32))
                        p_b2(K_sb, V_sb, ctx_sb, qh, probs, rbsb, psc, prow, pctx, craw)
                        ph.flush(block)
                        join_all()
                if next_phase():
                    with ExitStack() as s2:
                        wslab = [s2.enter_context(nc.sbuf_tensor(nm(f"ws{i}"), [128, HC * 128], BF16)) for i in range(2)]
                        rt2 = [s2.enter_context(nc.sbuf_tensor(nm(f"r2{i}"), [128, T], F32)) for i in range(2)]
                        pw = [s2.enter_context(nc.psum_tensor(nm(f"pw{i}"), [128, T], F32)) for i in range(2)]
                        p_b3(ctx_sb, wslab, rt2, pw)
                        ph.flush(block)
                        join_all()

        ffn_block(r_own, out)

        if phase_no[0] > phase_limit:
            tgts = tuple((s, s.n) for s in all_sems)
            dRS.plan(16)
            ph.on("scalar", lambda eng, tg=tgts: (_w(eng, tg),
                  eng.dma_start(out=out, in_=r_own).then_inc(dRS.h, 16)))

        tgts = tuple((s, s.n) for s in all_sems)
        for e in ENGS:
            ph.on(e, lambda eng, tg=tgts: _w(eng, tg))
        ph.flush(block)

    return nc



def _prep_weights(inp):
    gate = np.asarray(inp["L0_ffn_gate_wq"])
    up = np.asarray(inp["L0_ffn_up_wq"])
    down = np.asarray(inp["L0_ffn_down_wq"])
    wq = np.asarray(inp["L0_attn_Wq"], dtype=np.float32)
    wk = np.asarray(inp["L0_attn_Wk"], dtype=np.float32)
    wv = np.asarray(inp["L0_attn_Wv"], dtype=np.float32)
    wo = np.asarray(inp["L0_attn_Wo"], dtype=np.float32)

    d = {}
    d["gate_w"] = np.ascontiguousarray(
        gate.reshape(IC, 128, HC, 128).transpose(0, 3, 2, 1)).reshape(IC, 128, HC * 128)
    d["up_w"] = np.ascontiguousarray(
        up.reshape(IC, 128, HC, 128).transpose(0, 3, 2, 1)).reshape(IC, 128, HC * 128)
    dT = np.ascontiguousarray(down.T)
    d["down_w"] = np.ascontiguousarray(
        dT.reshape(2, 43, 128, 8, 4, 128).transpose(3, 0, 2, 1, 4, 5)).reshape(8, 2, 128, 43 * 512)

    def proj(w, noc):
        return np.ascontiguousarray(
            w.reshape(noc, 128, HC, 128).transpose(0, 3, 2, 1)).reshape(noc, 128, HC * 128)

    d["wq_w"] = proj(wq, NQH)
    d["wk_w"] = proj(wk, NKV)
    d["wv_w"] = proj(wv, NKV)
    d["wo_w"] = proj(wo, HC)
    d["gate_sc"] = np.ascontiguousarray(np.asarray(inp["L0_ffn_gate_scale"], np.float32).reshape(IC, 128).T)
    d["up_sc"] = np.ascontiguousarray(np.asarray(inp["L0_ffn_up_scale"], np.float32).reshape(IC, 128).T)
    d["down_sc"] = np.ascontiguousarray(np.asarray(inp["L0_ffn_down_scale"], np.float32).reshape(HC, 128).T)
    d["bv_c"] = np.ascontiguousarray(np.asarray(inp["L0_attn_bv"], np.float32).reshape(NKV, 128).T)
    d["ident_in"] = np.eye(128, dtype=np.float32)
    return d


def _fm(x_half):
    return np.ascontiguousarray(x_half.reshape(T, HC, 128).transpose(2, 1, 0)).reshape(128, HC * T)


def kernel(**inputs):
    x = np.asarray(inputs["L0_attn_hidden_states"], dtype=np.float32)
    shared = _prep_weights(inputs)
    nc = build_nc()
    in_maps = []
    for c in range(8):
        b, hf = c // 2, c % 2
        m = {"x_own": _fm(x[b, hf * T:(hf + 1) * T]),
             "x_par": _fm(x[b, (1 - hf) * T:(2 - hf) * T])}
        m.update(shared)
        in_maps.append(m)
    res = run_bass_kernel_spmd(nc, in_maps, core_ids=list(range(8)))
    outp = np.empty((BATCH, S, H), dtype=np.float32)
    for c in range(8):
        b, hf = c // 2, c % 2
        o = np.asarray(res.results[c]["out"]).reshape(128, HC, T)
        outp[b, hf * T:(hf + 1) * T] = o.transpose(2, 1, 0).reshape(T, H)
    return outp
